# version 04 (final)
# baseline (speedup 1.0000x reference)
import numpy as np
import concourse.bass as bass
import concourse.mybir as mybir
from concourse.bass_utils import run_bass_kernel_spmd
from contextlib import ExitStack

F32 = mybir.dt.float32
F32R = mybir.dt.float32r
BF16 = mybir.dt.bfloat16
I32 = mybir.dt.int32
AF = mybir.ActivationFunctionType
AL = mybir.AluOpType
AX = mybir.AxisListType

NCORES = 8
D = 1024
N64 = 64
FC = 8
TWO_PI = 6.283185307179586
INV_2PI = 1.0 / TWO_PI
HALF_PI = 1.5707963267948966
LN_EPS = 1e-5
TTK = 16
STAGE = 4
DBG_XS = 0
ENGS = ("sync", "tensor", "vector", "scalar", "gpsimd")
EINC = {"vector": "vv", "scalar": "aa", "gpsimd": "gg", "tensor": "mm"}


class Sched:
    def __init__(self):
        self.ops = {e: [] for e in ENGS}
        self.val = {}

    def tok(self, sem):
        return (sem, self.val.get(sem, 0))

    def barrier(self):
        return [self.tok(s) for s in ("mm", "vv", "aa", "gg", "ld", "wl", "tp", "sp")]

    STRENGTHEN = ("ld", "l2", "tp", "st", "sp", "wl")

    def op(self, eng, fn, waits=(), inc=None, n=1):
        waits = tuple((s, self.val.get(s, 0) if s in Sched.STRENGTHEN else v)
                      for (s, v) in waits if s is not None and v > 0)
        waits = tuple((s, v) for (s, v) in waits if v > 0)
        if inc is not None and n >= 16 and self.val.get(inc, 0) > 0 and \
                not any(s == inc for s, _ in waits):
            waits = waits + ((inc, self.val[inc]),)
        tokout = None
        if inc is not None:
            self.val[inc] = self.val.get(inc, 0) + n
            tokout = (inc, self.val[inc])
        self.ops[eng].append((waits, fn, inc, n))
        return tokout


def build_nc(TT):
    nc = bass.Bass("TRN2", target_bir_lowering=False, debug=False)
    S = Sched()

    def din(name, shape, dt=F32):
        return nc.dram_tensor(name, shape, dt, kind="ExternalInput").ap()

    X = din("x", [128, TT, D])
    XOUT = nc.dram_tensor("xout", [128, TT, D], F32, kind="ExternalOutput").ap()
    LRc = din("lrc", [N64, 1]); LIc = din("lic", [N64, 1])
    LRr = din("lrr", [1, N64]); LIr = din("lir", [1, N64])
    CRT = din("crt", [N64, D]); CIT = din("cit", [N64, D])
    DMAP = din("dmap", [128, D]); GMAP = din("gmap", [128, D]); BMAPA = din("bmapa", [128, D])
    FCM = din("fc129", [128, 129], F32R)
    FSM = din("fs128", [128, 128], F32R)
    WCM = din("wc", [128, 128]); WSM = din("ws", [128, 128]); WCX = din("wcx", [1, 128])
    DRAMP = din("dramp", [N64, 129])
    RAMPT = din("rampt", [128, N64])
    RAMP16 = din("ramp16c", [N64, 16])
    EOFF = din("eoff", [N64, NCORES])
    ONESR = din("ones_r", [1, 2048])
    ONESC = din("ones_c", [128, 1])
    WK = din("wk", [128, FC, D]); WQ = din("wq", [128, FC, D]); WV = din("wv", [128, FC, D])
    W1 = din("w1", [128, FC, D]); W2 = din("w2", [128, FC, D]); W2T = din("w2t", [128, FC, D])
    WLR = din("wlr", [128, FC, 1])
    BK = din("bk", [1, D]); BQ = din("bq", [1, D]); BV = din("bv", [1, D])
    B1 = din("b1", [1, D]); B2 = din("b2", [1, D]); BLRC = din("blrc", [128, 1])
    GWT = din("gwt", [128, FC, 8])
    EW = din("ew", [8, 128, FC, D])
    xspill = nc.dram_tensor("xspill", [128, TT, D], F32).ap()
    qspill = nc.dram_tensor("qspill", [128, FC, TT * 128], BF16).ap()
    ag_in = nc.dram_tensor("ag_in", [N64, 2 * D], F32).ap()
    ag_out = nc.dram_tensor("ag_out", [NCORES, N64, 2 * D], F32).ap()
    ar_in = nc.dram_tensor("ar_in", [128, 17, D], F32).ap()
    ar_out = nc.dram_tensor("ar_out", [128, 17, D], F32).ap()

    es = ExitStack()

    def sbuf(name, shape, dt=F32, scope=None):
        return (scope or es).enter_context(nc.sbuf_tensor(name, shape, dt))

    scr = [sbuf(f"scr{i}", [128, D]) for i in range(3)]
    bscr = [sbuf(f"bscr{i}", [128, D], BF16) for i in range(4)]
    col = [sbuf(f"col{i}", [128, 1]) for i in range(8)]
    alr2c = sbuf("alr2c", [128, TT])
    masks = sbuf("masks", [128, TT, 8])
    rmaxc = sbuf("rmaxc", [128, 1])
    epscol = sbuf("epscol", [128, 1])
    erow = sbuf("erow", [1, 128], F32R)
    onesrb = sbuf("onesrb", [1, 512], BF16)
    onesrr = sbuf("onesrr", [1, 128], F32R)
    onesc_s = sbuf("onesc_s", [128, 1])
    onescb = sbuf("onescb", [128, 1], BF16)
    blrc_s = sbuf("blrc_s", [128, 1])
    gws = sbuf("gws", [128, FC, 8])
    gwhb = sbuf("gwhb", [128, FC, 8], BF16); gwlb = sbuf("gwlb", [128, FC, 8], BF16)
    pr = sbuf("pr", [N64, 129]); pi = sbuf("pi", [N64, 129])
    aco = sbuf("aco", [N64, 1]); tho = sbuf("tho", [N64, 1])
    lamb = sbuf("lamb", [N64, 2])
    wbcol = sbuf("wbcol", [N64, 2 * NCORES])
    wlamk = sbuf("wlamk", [N64, 32])
    p2ck = sbuf("p2ck", [N64, 256], F32R)
    gcol = sbuf("gcol", [N64, 6])
    attr = sbuf("attr", [128, N64], F32R); atti = sbuf("atti", [128, N64], F32R)
    p2rr = sbuf("p2rr", [N64, 128], F32R); np2ir = sbuf("np2ir", [N64, 128], F32R)
    prr = sbuf("prr", [N64, 128], F32R); npir = sbuf("npir", [N64, 128], F32R)
    fcm_s = sbuf("fcm_s", [128, 129], F32R)
    fsm_s = sbuf("fsm_s", [128, 128], F32R)
    wcb = sbuf("wcb", [128, 128], BF16); wsb = sbuf("wsb", [128, 128], BF16)
    wcxb = sbuf("wcxb", [1, 128], BF16)
    P = [es.enter_context(nc.psum_tensor(f"P{i}", [128, D], F32)) for i in range(4)]
    NH = 4 if TT % 4 == 0 else 2
    TTH0 = TT // NH
    XS_KB = 4 * TT
    ST_KB = XS_KB
    HT_KB = ST_KB + 16
    KF_KB = HT_KB + 8
    KP_KB = KF_KB + 4
    YF_KB = KP_KB + 6
    DM_KB = YF_KB + 4
    GM_KB = DM_KB + 4
    BM_KB = GM_KB + 4
    A_TOP = BM_KB + 4
    G_B = 8 if TT % 8 == 0 else (2 if TT % 2 == 0 else 1)
    TTG_B = TT // G_B
    PSZ_B = FC * TTG_B * 128 * 2 // 1024
    B_TOP = 10 * PSZ_B + 24 + 48 + 11
    NQ = 4 if TT % 4 == 0 else 2
    TTQ = TT // NQ
    SZQ = FC * TTQ * 128 * 2 // 1024
    F_TOP = XS_KB + 3 * SZQ + 32
    ARENA_KB = max(A_TOP + 8, B_TOP, F_TOP, XS_KB + 2 * TT)
    R_KB = 4 * TTH0 + 20
    arena = sbuf("arena", [128, ARENA_KB * 1024], mybir.dt.uint8)
    arena_r = sbuf("arena_r", [128, R_KB * 1024], mybir.dt.uint8)

    def _carve(buf, cap_kb, off_b, shape, dt, plo):
        nbytes = int(np.prod(shape[1:])) * mybir.dt.size(dt)
        assert off_b + nbytes <= cap_kb * 1024, (off_b, shape, dt)
        v = buf[plo:plo + shape[0], off_b: off_b + nbytes].bitcast(dt)
        if len(shape) == 2:
            return v
        if len(shape) == 3:
            return v.rearrange("p (a b) -> p a b", a=shape[1])
        if len(shape) == 4:
            return v.rearrange("p (a b c) -> p a b c", a=shape[1], b=shape[2])
        raise ValueError(shape)

    def carve(off_kb, shape, dt=F32, plo=0):
        return _carve(arena, ARENA_KB, off_kb * 1024, shape, dt, plo)

    def carve_r(off_kb, shape, dt=F32, plo=0):
        return _carve(arena_r, R_KB, off_kb * 1024, shape, dt, plo)

    _scur = [A_TOP * 1024]

    def scarve(shape, dt=F32, plo=0):
        nbytes = int(np.prod(shape[1:])) * mybir.dt.size(dt)
        off = _scur[0]
        _scur[0] = off + ((nbytes + 31) // 32) * 32
        assert _scur[0] <= (A_TOP + 8) * 1024
        return _carve(arena, ARENA_KB, off, shape, dt, plo)

    lrow = scarve([1, 128])
    onesr_s = scarve([1, 128])
    pk = scarve([N64, 420])
    pki = scarve([N64, 150], I32)
    ramp16_s = scarve([N64, 16])
    amap = scarve([128, N64]); thmap = scarve([128, N64])
    mapt = scarve([128, N64]); mapi_ = scarve([128, N64], I32)
    maps_ = scarve([128, N64]); maps2_ = scarve([128, N64])
    artT = scarve([128, N64]); aitT = scarve([128, N64])
    wcm_s = scarve([128, 128]); wsm_s = scarve([128, 128])
    wcx_s = scarve([1, 128])
    dramp_s = scarve([N64, 129])
    rampt_s = scarve([128, N64])
    eoff_s = scarve([N64, NCORES])

    xs = carve(0, [128, TT, D], F32)
    sems = {s: es.enter_context(nc.semaphore(s)) for s in
            ["ld", "l2", "wl", "tp", "st", "mm", "vv", "aa", "gg", "cc", "sp",
             "w0", "w1", "w2", "w3"]}


    def dma_in(dst, src, waits=()):
        return S.op("sync", lambda e, d=dst, s=src: e.dma_start(out=d, in_=s),
                    waits=waits, inc="ld", n=16)

    def dma_in2(dst, src, waits=()):
        return S.op("sync", lambda e, d=dst, s=src: e.dma_start(out=d, in_=s),
                    waits=waits, inc="l2", n=16)

    def dma_st(dst, src, waits=()):
        return S.op("sync", lambda e, d=dst, s=src: e.dma_start(out=d, in_=s),
                    waits=waits, inc="st", n=16)

    def dma_sp(dst, src, waits=()):
        return S.op("scalar", lambda e, d=dst, s=src: e.dma_start(out=d, in_=s),
                    waits=waits, inc="sp", n=16)

    def dma_w(dst, src, waits=(), accum=False):
        if accum:
            return S.op("gpsimd",
                        lambda e, d=dst, s=src: e.dma_start(out=d, in_=s, accum_op=AL.add),
                        waits=waits, inc="wl", n=16)
        return S.op("gpsimd", lambda e, d=dst, s=src: e.dma_start(out=d, in_=s),
                    waits=waits, inc="wl", n=16)

    def dma_tp(dst, src, waits=()):
        return S.op("sync", lambda e, d=dst, s=src: e.dma_start(d, s, transpose=True),
                    waits=waits, inc="tp", n=16)

    def ve(fn, waits=(), eng="vector"):
        return S.op(eng, fn, waits=waits, inc=EINC[eng])

    def mm_group(fns, waits=()):
        def go(e):
            ins = None
            for f in fns:
                ins = f(e)
            return ins
        return S.op("tensor", go, waits=waits, inc="mm")

    def mmf(out_ap, lhsT_ap, rhs_ap, start, stop):
        return lambda e: e.matmul(out_ap, lhsT_ap, rhs_ap, start=start, stop=stop)

    pfree = [None, None, None, None]

    def pw(i):
        return [pfree[i]] if pfree[i] is not None else []

    t_x = dma_in(xs[:], X)
    ld = {}
    crt_cit = carve(ST_KB, [N64, 2 * D], F32)
    crt_s = crt_cit[:, 0:D]; cit_s = crt_cit[:, D:2 * D]
    gmap_s = carve(GM_KB, [128, D], F32)
    bmapa_s = carve(BM_KB, [128, D], F32)
    for key, dst, src in [
        ("lrc", aco[:], LRc),
    ]:
        pass
    lrc_s = pk[:, 0:1]; lic_s = pk[:, 1:2]
    for key, dst, src in [
        ("lrc", lrc_s, LRc), ("lic", lic_s, LIc),
        ("lrr", lrow[:, 0:N64], LRr), ("lir", lrow[:, 64:128], LIr),
        ("crt", crt_s, CRT), ("cit", cit_s, CIT),
        ("gmap", gmap_s, GMAP), ("bmapa", bmapa_s, BMAPA),
        ("fcm", fcm_s[:], FCM), ("fsm", fsm_s[:], FSM),
        ("wcm", wcm_s[:], WCM), ("wsm", wsm_s[:], WSM), ("wcx", wcx_s[:], WCX),
        ("dramp", dramp_s[:], DRAMP), ("rampt", rampt_s[:], RAMPT),
        ("ramp16", ramp16_s[:], RAMP16), ("eoff", eoff_s[:], EOFF),
        ("onesr", onesr_s[:], ONESR[:, 0:128]), ("onesc", onesc_s[:], ONESC),
        ("dmap", None, None),
        ("gwt", gws[:], GWT), ("blrc", blrc_s[:], BLRC),
    ]:
        if key == "dmap":
            dmap_s2 = carve(DM_KB, [128, D], F32)
            ld[key] = dma_in(dmap_s2, DMAP)
            continue
        ld[key] = dma_in(dst, src)
    t_onesrb = dma_w(onesrb[:], ONESR[:, 0:512], waits=[])
    t_onesrr = ve(lambda e: e.tensor_copy(onesrr[:], onesr_s[:]), waits=[ld["onesr"]])
    t_onescb = ve(lambda e: e.tensor_copy(onescb[:], onesc_s[:]), waits=[ld["onesc"]])
    t_wcb = ve(lambda e: e.tensor_copy(wcb[:], wcm_s[:]), waits=[ld["wcm"]])
    t_wsb = ve(lambda e: e.tensor_copy(wsb[:], wsm_s[:]), waits=[ld["wsm"]])
    t_wcxb = ve(lambda e: e.tensor_copy(wcxb[:], wcx_s[:]), waits=[ld["wcx"]])

    ln_prev = [None]

    def emit_ln(src_ap, w_src, dummy=None):
        du = dummy if dummy is not None else bscr[3][:]
        su, sq, mu, r, mr = (c[:] for c in col[0:5])
        ta = ve(lambda e: e.activation(du, src_ap, AF.Copy, accum_out=su),
                waits=list(w_src) + ([ln_prev[0]] if ln_prev[0] else []), eng="scalar")
        tb = ve(lambda e: e.activation(du, src_ap, AF.Square, accum_out=sq),
                waits=[ta], eng="scalar")
        t1 = ve(lambda e: e.tensor_scalar(mu, su, 1.0 / D, None, AL.mult), waits=[ta])
        t2 = ve(lambda e: e.tensor_tensor(su, mu, mu, AL.mult), waits=[t1])
        t3 = ve(lambda e: e.scalar_tensor_tensor(sq, sq, 1.0 / D, su, AL.mult, AL.subtract),
                waits=[tb, t2])
        t3b = ve(lambda e: e.tensor_scalar(sq, sq, 1.0, LN_EPS, AL.mult, AL.add),
                 waits=[t3])
        t4 = ve(lambda e: e.activation(su, sq, AF.Sqrt), waits=[t3b], eng="scalar")
        t5 = ve(lambda e: e.reciprocal(r, su), waits=[t4])
        t6 = ve(lambda e: e.tensor_tensor(mr, mu, r, AL.mult), waits=[t5])
        return t6, r, mr

    def emit_powers(ramp_ap, a_ap, th_ap, out_r, out_i, tmp, tmpi, tmps, waits,
                    scalar_mode=True, tmps2=None):
        if tmps2 is None:
            tmps2 = tmpi
        def mul(dst, src, s_ap):
            if scalar_mode:
                return lambda e: e.tensor_scalar(dst, src, s_ap, None, AL.mult)
            return lambda e: e.tensor_tensor(dst, src, s_ap, AL.mult)
        if scalar_mode:
            t1 = ve(lambda e: e.tensor_scalar(out_r, ramp_ap, a_ap, -1.0, AL.mult,
                                              AL.mult), waits=waits)
        else:
            t1a = ve(mul(out_r, ramp_ap, a_ap), waits=waits)
            t1 = ve(lambda e: e.tensor_scalar(out_r, out_r, -1.0, None, AL.mult),
                    waits=[t1a])
        t2 = ve(lambda e: e.activation(out_r, out_r, AF.Exp), waits=[t1], eng="scalar")
        t3 = ve(mul(out_i, ramp_ap, th_ap), waits=waits)

        def mod_reduce(src_ap, mask_ap, w):
            m1 = ve(lambda e: e.tensor_scalar(tmp, src_ap, INV_2PI, None, AL.mult),
                    waits=w)
            m2 = ve(lambda e: e.tensor_copy(tmpi, tmp), waits=[m1])
            m3 = ve(lambda e: e.tensor_copy(tmp, tmpi), waits=[m2])
            m4 = ve(lambda e: e.scalar_tensor_tensor(tmp, tmp, -TWO_PI, src_ap, AL.mult,
                                                     AL.add), waits=[m3])
            m5 = ve(lambda e: e.tensor_scalar(mask_ap, tmp, 3.14159274, None, AL.is_gt),
                    waits=[m4])
            m6 = ve(lambda e: e.scalar_tensor_tensor(tmp, mask_ap, -TWO_PI, tmp, AL.mult,
                                                     AL.add), waits=[m5])
            m7 = ve(lambda e: e.tensor_scalar(tmp, tmp, 3.14159265, -3.14159265,
                                              AL.min, AL.max), waits=[m6])
            return m7

        t7 = mod_reduce(out_i, tmps, [t3])
        t8 = ve(lambda e: e.activation(tmps, tmp, AF.Sin), waits=[t7], eng="scalar")
        t8b = ve(lambda e: e.tensor_tensor(tmps, tmps, out_r, AL.mult), waits=[t8, t2])
        t9 = ve(lambda e: e.tensor_scalar(out_i, out_i, 1.0, HALF_PI, AL.mult, AL.add),
                waits=[t8])
        t13 = mod_reduce(out_i, tmps2, [t9])
        t14 = ve(lambda e: e.activation(tmp, tmp, AF.Sin), waits=[t13], eng="scalar")
        t16 = ve(lambda e: e.tensor_tensor(out_r, out_r, tmp, AL.mult), waits=[t14, t8b])
        t17 = ve(lambda e: e.tensor_copy(out_i, tmps), waits=[t16, t13])
        return t17

    TTh0 = TTH0
    xn = carve_r(0, [128, TTh0, D], F32R)
    sTab = carve(ST_KB, [N64, 4 * D], F32)
    Sa, Sb = sTab[:, 0:2 * D], sTab[:, 2 * D:4 * D]
    tTab = carve_r(4 * TTh0 + 4, [N64, 2 * D], F32R)
    Tr, Ti = tTab[:, 0:D], tTab[:, D:2 * D]
    hTab = carve(HT_KB, [N64, 2 * D], F32)
    hsc = [hTab[:, 0:D], hTab[:, D:2 * D]]
    cTab2 = carve_r(4 * TTh0 + 12, [N64, 2 * D], F32R)
    ccr, cci = cTab2[:, 0:D], cTab2[:, D:2 * D]
    ccrr, ccir = ccr, cci
    kc = carve_r(4 * TTh0, [128, D], F32R)
    kfrb = carve(KF_KB, [128, D], BF16)
    kfib = carve(KF_KB + 2, [128, D], BF16)
    kpack = carve(KP_KB, [1, 3 * D], BF16)
    k128b = kpack[:, 0:D]; x128b = kpack[:, D:2 * D]; y128b = kpack[:, 2 * D:3 * D]
    yfab = carve(YF_KB, [128, 2 * D], BF16)
    yfr = yfab[:, 0:D]; yfi = yfab[:, D:2 * D]
    mg = carve(KP_KB, [N64, 2 * D], F32)

    t_a = ve(lambda e: e.activation(aco[:], lrc_s, AF.Exp), waits=[ld["lrc"]], eng="scalar")
    t_th = ve(lambda e: e.activation(tho[:], lic_s, AF.Exp), waits=[ld["lic"]], eng="scalar")
    t_P = emit_powers(dramp_s[:], aco[:], tho[:], pr[:], pi[:], pk[:, 2:131],
                      pki[:, 0:129], pk[:, 131:260], [ld["dramp"], t_a, t_th],
                      tmps2=pk[:, 260:389])
    t_lb = ve(lambda e: e.tensor_copy(lamb[:, 0:1], pr[:, 128:129]), waits=[t_P])
    t_lb2 = ve(lambda e: e.tensor_copy(lamb[:, 1:2], pi[:, 128:129]), waits=[t_P])
    t_prr = ve(lambda e: e.tensor_copy(prr[:], pr[:, 0:128]), waits=[t_P])
    t_npir = ve(lambda e: e.tensor_scalar(npir[:], pi[:, 0:128], -1.0, None, AL.mult),
                waits=[t_P])
    t_p2rr = ve(lambda e: e.tensor_copy(p2rr[:], pr[:, 1:129]), waits=[t_P])
    t_np2ir = ve(lambda e: e.tensor_scalar(np2ir[:], pi[:, 1:129], -1.0, None, AL.mult),
                 waits=[t_P])
    gden, grc, gic, gu, gv, gt = (gcol[:, i:i + 1] for i in range(6))
    t_d1 = ve(lambda e: e.tensor_tensor(gden, aco[:], aco[:], AL.mult), waits=[t_a, t_P])
    t_d2 = ve(lambda e: e.tensor_tensor(gt, tho[:], tho[:], AL.mult), waits=[t_th, t_P])
    t_d3 = ve(lambda e: e.tensor_tensor(gden, gden, gt, AL.add), waits=[t_d1, t_d2])
    t_d4 = ve(lambda e: e.reciprocal(gden, gden), waits=[t_d3])
    t_u = ve(lambda e: e.tensor_scalar(gu, pr[:, 1:2], 1.0, -1.0, AL.mult, AL.add),
             waits=[t_P])
    t_v = ve(lambda e: e.tensor_copy(gv, pi[:, 1:2]), waits=[t_P])
    t_g1 = ve(lambda e: e.tensor_tensor(grc, gu, aco[:], AL.mult), waits=[t_u])
    t_g2 = ve(lambda e: e.tensor_tensor(gt, gv, tho[:], AL.mult), waits=[t_v, t_d3])
    t_g3 = ve(lambda e: e.tensor_tensor(grc, gt, grc, AL.subtract), waits=[t_g1, t_g2])
    t_g4 = ve(lambda e: e.tensor_tensor(grc, grc, gden, AL.mult), waits=[t_g3, t_d4])
    t_g5 = ve(lambda e: e.tensor_tensor(gic, gv, aco[:], AL.mult), waits=[t_v, t_g4])
    t_g6 = ve(lambda e: e.tensor_tensor(gt, gu, tho[:], AL.mult), waits=[t_g5, t_g3])
    t_g7 = ve(lambda e: e.tensor_tensor(gic, gic, gt, AL.add), waits=[t_g6])
    t_g8 = ve(lambda e: e.tensor_scalar(gic, gic, -1.0, None, AL.mult), waits=[t_g7])
    t_g9 = ve(lambda e: e.tensor_tensor(gic, gic, gden, AL.mult), waits=[t_g8])
    t_c1 = ve(lambda e: e.tensor_scalar(ccr, crt_s, grc, None, AL.mult),
              waits=[ld["crt"], t_g4])
    t_c2 = ve(lambda e: e.tensor_scalar(hsc[0], cit_s, gic, None, AL.mult),
              waits=[ld["cit"], t_g9])
    t_c3 = ve(lambda e: e.tensor_tensor(ccr, ccr, hsc[0], AL.subtract), waits=[t_c1, t_c2])
    t_c4 = ve(lambda e: e.tensor_scalar(cci, crt_s, gic, None, AL.mult), waits=[t_g9])
    t_c5 = ve(lambda e: e.tensor_scalar(hsc[1], cit_s, grc, None, AL.mult), waits=[t_g4])
    t_c6 = ve(lambda e: e.tensor_tensor(cci, cci, hsc[1], AL.add), waits=[t_c4, t_c5])
    t_K = mm_group(
        [mmf(P[0][:, n * 512:(n + 1) * 512], prr[:], ccrr[:, n * 512:(n + 1) * 512],
             True, False) for n in (0, 1)] +
        [mmf(P[0][:, n * 512:(n + 1) * 512], npir[:], ccir[:, n * 512:(n + 1) * 512],
             False, True) for n in (0, 1)],
        waits=[t_prr, t_npir, t_c3, t_c6] + pw(0))
    t_Kev = ve(lambda e: e.tensor_copy(kc[:], P[0][:]), waits=[t_K])
    pfree[0] = t_Kev
    t_Kf1 = mm_group([mmf(P[0][:, n * 512:(n + 1) * 512], fcm_s[:, 0:128],
                          kc[:, n * 512:(n + 1) * 512], True, True) for n in (0, 1)],
                     waits=[t_Kev, ld["fcm"]] + pw(0))
    t_Kf2 = mm_group([mmf(P[1][:, n * 512:(n + 1) * 512], fsm_s[:],
                          kc[:, n * 512:(n + 1) * 512], True, True) for n in (0, 1)],
                     waits=[t_Kev, ld["fsm"]] + pw(1))
    t_Kf3 = mm_group([mmf(P[2][0:1, n * 512:(n + 1) * 512], fcm_s[:, 128:129],
                          kc[:, n * 512:(n + 1) * 512], True, True) for n in (0, 1)],
                     waits=[t_Kev] + pw(2))
    t_kfr = ve(lambda e: e.copy(kfrb[:], P[0][:]), waits=[t_Kf1], eng="scalar")
    t_kfi = ve(lambda e: e.copy(kfib[:], P[1][:]), waits=[t_Kf2], eng="scalar")
    t_k128 = ve(lambda e: e.copy(k128b, P[2][0:1, :]), waits=[t_Kf3], eng="scalar")
    pfree[0], pfree[1], pfree[2] = t_kfr, t_kfi, t_k128
    t_ar_row = ve(lambda e: e.activation(erow[:, 0:N64], lrow[:, 0:N64], AF.Exp),
                  waits=[ld["lrr"]], eng="scalar")
    t_th_row = ve(lambda e: e.activation(erow[:, 64:128], lrow[:, 64:128], AF.Exp),
                  waits=[ld["lir"]], eng="scalar")
    t_am = mm_group([mmf(P[3][:, 0:64], onesrr[:, 0:128], erow[:, 0:N64], True, True)],
                    waits=[t_onesrr, t_ar_row] + pw(3))
    t_amev = ve(lambda e: e.tensor_copy(amap[:], P[3][:, 0:64]), waits=[t_am])
    pfree[3] = t_amev
    t_tm = mm_group([mmf(P[3][:, 0:64], onesrr[:, 0:128], erow[:, 64:128], True, True)],
                    waits=[t_th_row, t_amev] + pw(3))
    t_tmev = ve(lambda e: e.tensor_copy(thmap[:], P[3][:, 0:64]), waits=[t_tm])
    pfree[3] = t_tmev
    t_Arev = emit_powers(rampt_s[:], amap[:], thmap[:], artT[:], aitT[:], mapt[:],
                         mapi_[:], maps_[:], [ld["rampt"], t_amev, t_tmev],
                         scalar_mode=False, tmps2=maps2_[:])
    wkr = wlamk[:, 0:16]; wki = wlamk[:, 16:32]
    t_wk16 = emit_powers(ramp16_s[:], aco[:], tho[:], wkr, wki, pk[:, 2:18],
                         pki[:, 0:16], pk[:, 131:147], [ld["ramp16"], t_P],
                         tmps2=pk[:, 147:163])
    t_s0r = ve(lambda e: e.memset(Sa[:, 0:D], 0.0), waits=[t_c3, t_c6])
    t_s0i = ve(lambda e: e.memset(Sa[:, D:2 * D], 0.0), waits=[t_c3, t_c6], eng="gpsimd")
    stok_r, stok_i = t_s0r, t_s0i

    def emit_T(s_ap, w_r, w_i):
        w_r = w_r if isinstance(w_r, list) else [w_r]
        w_i = w_i if isinstance(w_i, list) else [w_i]
        v1 = ve(lambda e: e.tensor_tensor(hsc[0], s_ap[:, 0:D], ccr, AL.mult),
                waits=w_r + [t_c3])
        v2 = ve(lambda e: e.tensor_tensor(hsc[1], s_ap[:, D:2 * D], cci, AL.mult),
                waits=w_i + [t_c6], eng="gpsimd")
        v3 = ve(lambda e: e.tensor_tensor(Tr, hsc[0], hsc[1], AL.subtract),
                waits=[v1, v2])
        w1_ = ve(lambda e: e.tensor_tensor(hsc[0], s_ap[:, 0:D], cci, AL.mult),
                 waits=[v3, t_c6])
        w2_ = ve(lambda e: e.tensor_tensor(hsc[1], s_ap[:, D:2 * D], ccr, AL.mult),
                 waits=[v3, t_c3], eng="gpsimd")
        w3_ = ve(lambda e: e.tensor_tensor(Ti, hsc[0], hsc[1], AL.add),
                 waits=[w1_, w2_], eng="gpsimd")
        return v3, w3_
    t_Tr, t_Ti = emit_T(Sa, stok_r, stok_i)
    xn_tok = [None] * TT
    xn_cons = [None] * TTh0
    prev_tb, prev_tc, prev_td = None, None, None
    xs_tok = [t_x] * TT
    sv = [Sa, Sb]
    t_attr0 = ve(lambda e: e.tensor_copy(attr[:], artT[:]), waits=[t_Arev])
    t_atti0 = ve(lambda e: e.tensor_copy(atti[:], aitT[:]), waits=[t_Arev])
    for half in range(NH):
        for tt in range(half * TTh0, (half + 1) * TTh0):
            j = tt - half * TTh0
            tok, r1, mr1 = emit_ln(xs[:, tt, :],
                                   [t_x] + ([prev_tc] if prev_tc else []),
                                   dummy=scr[1][:])
            ta = ve(lambda e, r1=r1, mr1=mr1, tt=tt:
                    e.tensor_scalar(scr[0][:], xs[:, tt, :], r1, mr1, AL.mult,
                                    AL.subtract),
                    waits=[tok] + ([prev_td] if prev_td else []) +
                    (list(xn_cons[j]) if xn_cons[j] else []))
            ln_prev[0] = ta
            tok2, r2, mr2 = emit_ln(scr[0][:], [ta], dummy=scr[1][:])
            tb = ve(lambda e, r2=r2, mr2=mr2:
                    e.tensor_scalar(scr[1][:], scr[0][:], r2, mr2, AL.mult, AL.subtract),
                    waits=[tok2] + ([prev_tc] if prev_tc else []))
            ln_prev[0] = tb
            tc = ve(lambda e: e.tensor_tensor(scr[0][:], scr[1][:], gmap_s, AL.mult),
                    waits=[tb, ld["gmap"]] + ([prev_td] if prev_td else []), eng="gpsimd")
            td = ve(lambda e, tt=tt, j=j: e.tensor_tensor(xn[:, j, :], scr[0][:],
                                                          bmapa_s, AL.add),
                    waits=[tc, ld["bmapa"]], eng="gpsimd")
            prev_tb, prev_tc, prev_td = tb, tc, td
            xn_tok[tt] = td

        for ck in range(half * TTh0, (half + 1) * TTh0):
            j = ck - half * TTh0
            cur = sv[ck % 2]
            nxt = sv[(ck + 1) % 2]
            t_xr = mm_group([mmf(P[0][:, n * 512:(n + 1) * 512], fcm_s[:, 0:128],
                                 xn[:, j, n * 512:(n + 1) * 512], True, True) for n in (0, 1)],
                            waits=[xn_tok[ck], ld["fcm"]] + pw(0))
            t_xi = mm_group([mmf(P[1][:, n * 512:(n + 1) * 512], fsm_s[:],
                                 xn[:, j, n * 512:(n + 1) * 512], True, True) for n in (0, 1)],
                            waits=[xn_tok[ck]] + pw(1))
            t_x1 = mm_group([mmf(P[2][0:1, n * 512:(n + 1) * 512], fcm_s[:, 128:129],
                                 xn[:, j, n * 512:(n + 1) * 512], True, True) for n in (0, 1)],
                            waits=[xn_tok[ck]] + pw(2))
            t_exr = ve(lambda e: e.copy(bscr[0][:], P[0][:]), waits=[t_xr], eng="scalar")
            t_exi = ve(lambda e: e.copy(bscr[1][:], P[1][:]), waits=[t_xi], eng="scalar")
            t_ex1 = ve(lambda e: e.copy(x128b, P[2][0:1, :]), waits=[t_x1], eng="scalar")
            f1 = ve(lambda e: e.tensor_tensor(bscr[2][:], bscr[0][:], kfrb[:], AL.mult),
                    waits=[t_exr, t_kfr])
            f2 = ve(lambda e: e.tensor_tensor(bscr[3][:], bscr[1][:], kfib[:], AL.mult),
                    waits=[t_exi, t_kfi], eng="gpsimd")
            f3 = ve(lambda e: e.tensor_tensor(yfr, bscr[2][:], bscr[3][:], AL.subtract),
                    waits=[f1, f2])
            f4 = ve(lambda e: e.tensor_tensor(bscr[2][:], bscr[0][:], kfib[:], AL.mult),
                    waits=[t_exr, t_kfi, f3], eng="gpsimd")
            f5 = ve(lambda e: e.tensor_tensor(bscr[3][:], bscr[1][:], kfrb[:], AL.mult),
                    waits=[t_exi, t_kfr, f3])
            f6 = ve(lambda e: e.tensor_tensor(yfi, bscr[2][:], bscr[3][:], AL.add),
                    waits=[f4, f5], eng="gpsimd")
            f7 = ve(lambda e: e.tensor_tensor(y128b, x128b, k128b, AL.mult),
                    waits=[t_ex1, t_k128])
            t_mr = mm_group([mmf(P[0][0:N64, n * 512:(n + 1) * 512], attr[:],
                                 xn[:, j, n * 512:(n + 1) * 512], True, True) for n in (0, 1)],
                            waits=[t_attr0, xn_tok[ck], f1, f5, t_exr])
            t_mi = mm_group([mmf(P[1][0:N64, n * 512:(n + 1) * 512], atti[:],
                                 xn[:, j, n * 512:(n + 1) * 512], True, True) for n in (0, 1)],
                            waits=[t_atti0, xn_tok[ck], f2, f3, f6, t_exi])
            y_fns = (
                [mmf(P[2][:, n * 512:(n + 1) * 512], wcb[:], yfr[:, n * 512:(n + 1) * 512],
                     True, False) for n in (0, 1)] +
                [mmf(P[2][:, n * 512:(n + 1) * 512], wsb[:], yfi[:, n * 512:(n + 1) * 512],
                     False, False) for n in (0, 1)] +
                [mmf(P[2][:, n * 512:(n + 1) * 512], wcxb[:], y128b[:, n * 512:(n + 1) * 512],
                     False, False) for n in (0, 1)] +
                [mmf(P[2][:, n * 512:(n + 1) * 512], p2rr[:], Tr[:, n * 512:(n + 1) * 512],
                     False, False) for n in (0, 1)] +
                [mmf(P[2][:, n * 512:(n + 1) * 512], np2ir[:], Ti[:, n * 512:(n + 1) * 512],
                     False, True) for n in (0, 1)])
            t_y = mm_group(y_fns, waits=[f3, f6, f7, t_Tr, t_Ti, t_ex1, t_wcb, t_wsb, t_wcxb,
                                         t_p2rr, t_np2ir])
            br = lamb[:, 0:1]; bi = lamb[:, 1:2]
            p1_ = ve(lambda e, cur=cur, bi=bi: e.scalar_tensor_tensor(
                hsc[0], cur[:, D:2 * D], bi, P[0][0:N64, :], AL.mult, AL.subtract),
                waits=[t_mr, t_lb2, t_Tr, t_Ti, stok_r])
            p2_ = ve(lambda e, cur=cur, nxt=nxt, br=br: e.scalar_tensor_tensor(
                nxt[:, 0:D], cur[:, 0:D], br, hsc[0], AL.mult, AL.subtract),
                waits=[p1_, t_lb])
            p3_ = ve(lambda e, cur=cur, bi=bi: e.scalar_tensor_tensor(
                hsc[1], cur[:, 0:D], bi, P[1][0:N64, :], AL.mult, AL.add),
                waits=[t_mi, t_lb2, t_Ti, t_Tr, stok_i])
            p4_ = ve(lambda e, cur=cur, nxt=nxt, br=br: e.scalar_tensor_tensor(
                nxt[:, D:2 * D], cur[:, D:2 * D], br, hsc[1], AL.mult, AL.add),
                waits=[p3_, t_lb])
            pfree[0], pfree[1] = p2_, p4_
            stok_r, stok_i = p2_, p4_
            d1 = ve(lambda e, ck=ck, j=j: e.tensor_tensor(scr[2][:],
                                                          xn[:, j, :].bitcast(F32),
                                                          dmap_s2, AL.mult),
                    waits=[xn_tok[ck], ld["dmap"]] +
                    ([xs_tok[ck - 1]] if ck > 0 else []), eng="gpsimd")
            d2 = ve(lambda e: e.scalar_tensor_tensor(scr[2][:], P[2][:], 1.0, scr[2][:],
                                                     AL.mult, AL.add),
                    waits=[t_y, d1])
            d3 = ve(lambda e, ck=ck: e.tensor_tensor(xs[:, ck, :], xs[:, ck, :],
                                                     scr[2][:], AL.add),
                    waits=[d2, xs_tok[ck]])
            pfree[2] = d2
            xs_tok[ck] = d3
            xn_cons[j] = (t_mi, d1)
            last_ty = t_y
            if ck < TT - 1:
                t_Tr, t_Ti = emit_T(nxt, [stok_r, t_y], [stok_i, t_y])



    s_fin = sv[TT % 2]
    t_agw = dma_w(ag_in[:, :], s_fin, waits=[stok_r, stok_i])
    t_cc = S.op("gpsimd",
                lambda e: e.collective_compute(
                    "AllGather", AL.bypass, replica_groups=[list(range(NCORES))],
                    ins=[ag_in.opt()], outs=[ag_out.opt()]),
                waits=[t_agw], inc="cc")
    wbr = wbcol[:, 0:NCORES]; wbi = wbcol[:, NCORES:2 * NCORES]
    t_wb = emit_powers(eoff_s[:], aco[:], tho[:], wbr, wbi, pk[:, 2:2 + NCORES],
                       pki[:, 0:NCORES], pk[:, 131:131 + NCORES],
                       [ld["eoff"], t_P, t_wk16], tmps2=pk[:, 139:147])
    Si_ = sv[(TT + 1) % 2]
    t_i0r = ve(lambda e: e.memset(Si_[:, 0:D], 0.0), waits=[stok_r, t_agw])
    t_i0i = ve(lambda e: e.memset(Si_[:, D:2 * D], 0.0), waits=[stok_i, t_agw],
               eng="gpsimd")
    itok_r, itok_i = t_i0r, t_i0i
    for b in range(NCORES):
        t_mb = dma_in2(mg[:], ag_out[b, :, :], waits=[t_cc, itok_r, itok_i])
        wb_r = wbr[:, b:b + 1]; wb_i = wbi[:, b:b + 1]
        u1 = ve(lambda e, wb_i=wb_i: e.tensor_scalar(hsc[0], mg[:, D:2 * D], wb_i, None,
                                                     AL.mult), waits=[t_mb, t_wb])
        u2 = ve(lambda e, wb_r=wb_r: e.scalar_tensor_tensor(hsc[0], mg[:, 0:D], wb_r,
                                                            hsc[0], AL.mult, AL.subtract),
                waits=[u1])
        u3 = ve(lambda e: e.tensor_tensor(Si_[:, 0:D], Si_[:, 0:D], hsc[0], AL.add),
                waits=[u2])
        v1 = ve(lambda e, wb_i=wb_i: e.tensor_scalar(hsc[1], mg[:, 0:D], wb_i, None,
                                                     AL.mult), waits=[t_mb, t_wb])
        v2 = ve(lambda e, wb_r=wb_r: e.scalar_tensor_tensor(hsc[1], mg[:, D:2 * D], wb_r,
                                                            hsc[1], AL.mult, AL.add),
                waits=[v1])
        v3 = ve(lambda e: e.tensor_tensor(Si_[:, D:2 * D], Si_[:, D:2 * D], hsc[1],
                                          AL.add), waits=[v2], eng="gpsimd")
        itok_r, itok_i = u3, v3
    t_Tir, t_Tii = emit_T(Si_, [itok_r, last_ty], [itok_i, last_ty])
    prevc = None
    for ck in range(TT):
        wr_c = wkr[:, ck:ck + 1]; wi_c = wki[:, ck:ck + 1]
        w0 = [t_P, t_wk16, t_Tir, t_Tii] + ([prevc] if prevc else [])
        c1 = ve(lambda e, wr_c=wr_c: e.tensor_scalar(mapt[:, 0:64].bitcast(F32)
                                                     if False else pk[:, 150:278],
                                                     pr[:, 1:129], wr_c, None, AL.mult),
                waits=w0)
        c2 = ve(lambda e, wi_c=wi_c: e.scalar_tensor_tensor(pk[:, 150:278], pi[:, 1:129],
                                                            wi_c, pk[:, 150:278],
                                                            AL.mult, AL.subtract),
                waits=[c1])
        c3 = ve(lambda e: e.tensor_scalar(p2ck[:, 0:128], pk[:, 150:278], -1.0, None,
                                          AL.mult), waits=[c2])
        c4 = ve(lambda e, wi_c=wi_c: e.tensor_scalar(pk[:, 150:278], pr[:, 1:129], wi_c,
                                                     -1.0, AL.mult, AL.mult), waits=[c3])
        c5 = ve(lambda e, wr_c=wr_c: e.scalar_tensor_tensor(p2ck[:, 128:256], pi[:, 1:129],
                                                            wr_c, pk[:, 150:278], AL.mult,
                                                            AL.subtract), waits=[c4])
        c6 = ve(lambda e: e.tensor_scalar(p2ck[:, 128:256], p2ck[:, 128:256], -1.0, None,
                                          AL.mult), waits=[c5])
        t_ymm = mm_group(
            [mmf(P[2][:, n * 512:(n + 1) * 512], p2ck[:, 0:128],
                 Tr[:, n * 512:(n + 1) * 512], True, False) for n in (0, 1)] +
            [mmf(P[2][:, n * 512:(n + 1) * 512], p2ck[:, 128:256],
                 Ti[:, n * 512:(n + 1) * 512], False, True) for n in (0, 1)],
            waits=[c3, c6, t_Tir, t_Tii] + pw(2))
        d3c = ve(lambda e, ck=ck: e.scalar_tensor_tensor(xs[:, ck, :], P[2][:], 1.0,
                                                         xs[:, ck, :], AL.mult, AL.add),
                 waits=[t_ymm, xs_tok[ck]])
        pfree[2] = d3c
        xs_tok[ck] = d3c
        prevc = c6



    TTh = TTQ

    def Sv(buf):
        return buf[:].rearrange("p (c s f) -> p c s f", c=FC, s=TTh)

    def Fv(buf):
        return buf[:].rearrange("p (c t) -> p c t", c=FC)

    def transpose_set(slab_buf, f_buf, slab_toks):
        toks = []
        for fc in range(FC):
            toks.append(dma_tp(
                Fv(f_buf)[:, fc, 0:TTh * 128].rearrange("p (s f) -> p s f", s=TTh),
                Sv(slab_buf)[:, fc, 0:TTh, :],
                waits=list(slab_toks)))
        return toks

    gsplit_tok = [None]

    def emit_ffn(call_idx, xs):
        nonlocal xs_tok
        bar = S.barrier()
        sz = SZQ
        xf_sl = carve(XS_KB, [128, FC * TTh * 128], BF16)
        xl_sl = carve(XS_KB + sz, [128, FC * TTh * 128], BF16)
        xf_F = carve(XS_KB + sz, [128, FC * TTh * 128], BF16)
        xl_F = carve(XS_KB + 2 * sz, [128, FC * TTh * 128], BF16)
        ew_b = [carve(XS_KB + 3 * sz + 16 * i, [128, FC, D], BF16) for i in range(2)]
        if gsplit_tok[0] is None:
            g1 = ve(lambda e: e.tensor_copy(gwhb[:], gws[:]), waits=[ld["gwt"]])
            g2 = ve(lambda e: e.scalar_tensor_tensor(gwlb[:], gwhb[:], -1.0, gws[:],
                                                     AL.mult, AL.add), waits=[g1])
            gsplit_tok[0] = g2
        wfree = [None, None]
        for hf in range(NQ):
            base = hf * TTh
            sl_last, prev_tx = [], None
            for tti in range(TTh):
                tt = base + tti
                tok, r_, mr_ = emit_ln(xs[:, tt, :],
                                       [xs_tok[tt]] + ([prev_tx] if prev_tx else []))
                ta = ve(lambda e, tt=tt, r_=r_, mr_=mr_:
                        e.tensor_scalar(scr[0][:], xs[:, tt, :], r_, mr_, AL.mult,
                                        AL.subtract),
                        waits=[tok] + ([prev_tx] if prev_tx else []))
                ln_prev[0] = ta
                th_ = ve(lambda e, tti=tti: e.tensor_copy(
                    Sv(xf_sl)[:, :, tti, :],
                    scr[0][:].rearrange("p (c f) -> p c f", c=FC)),
                    waits=[ta] + list(bar) +
                    [t for t in wfree if t is not None])
                tl_ = ve(lambda e, tti=tti: e.scalar_tensor_tensor(
                    Sv(xl_sl)[:, :, tti, :], Sv(xf_sl)[:, :, tti, :], -1.0,
                    scr[0][:].rearrange("p (c f) -> p c f", c=FC), AL.mult, AL.add),
                    waits=[th_])
                tx = ve(lambda e, tt=tt: e.tensor_tensor(xs[:, tt, :], xs[:, tt, :],
                                                         scr[0][:], AL.add),
                        waits=[tl_, xs_tok[tt]])
                xs_tok[tt] = tx
                prev_tx = tx
                sl_last.append(tl_)
            tpl = transpose_set(xl_sl, xl_F, sl_last)
            tpf = transpose_set(xf_sl, xf_F, sl_last + tpl)
            prev_mk = None
            msk_toks = []
            for tti in range(TTh):
                tt = base + tti
                fns = []
                for k in range(FC):
                    lhsh = Fv(xf_F)[:, k, tti * 128:(tti + 1) * 128]
                    lhsl = Fv(xl_F)[:, k, tti * 128:(tti + 1) * 128]
                    fns.append(mmf(P[2][:, 0:8], lhsh, gwhb[:, k, :], k == 0, False))
                    fns.append(mmf(P[2][:, 0:8], lhsl, gwhb[:, k, :], False, False))
                    fns.append(mmf(P[2][:, 0:8], lhsh, gwlb[:, k, :], False, k == FC - 1))
                tok = mm_group(fns, waits=tpf + tpl + [gsplit_tok[0]] + pw(2))
                t_rm = ve(lambda e: e.reduce_max(rmaxc[:], P[2][:, 0:8], axis=AX.X),
                          waits=[tok] + ([prev_mk] if prev_mk else []))
                t_mk = ve(lambda e, tt=tt: e.tensor_scalar(masks[:, tt, :], P[2][:, 0:8],
                                                           rmaxc[:], None, AL.is_ge),
                          waits=[t_rm])
                pfree[2] = t_mk
                prev_mk = t_mk
                msk_toks.append(t_mk)
            for e_i in range(8):
                slot = e_i % 2
                t_we = S.op("gpsimd",
                            lambda e, slot=slot, e_i=e_i: e.dma_start(
                                out=ew_b[slot][:], in_=EW[e_i, :, :, :]),
                            waits=([wfree[slot]] if wfree[slot] else list(bar)),
                            inc=f"w{slot}", n=16)
                last_mm = None
                for tti in range(TTh):
                    tt = base + tti
                    pidx = tti % 2
                    fns = []
                    for k in range(FC):
                        lhsT = Fv(xf_F)[:, k, tti * 128:(tti + 1) * 128]
                        for nn in (0, 1):
                            fns.append(mmf(P[pidx][:, nn * 512:(nn + 1) * 512], lhsT,
                                           ew_b[slot][:, k, nn * 512:(nn + 1) * 512],
                                           k == 0, k == FC - 1))
                    tok = mm_group(fns, waits=tpf + [t_we, msk_toks[tti]] + pw(pidx))
                    tacc = ve(lambda e, tt=tt, pidx=pidx, e_i=e_i:
                              e.scalar_tensor_tensor(xs[:, tt, :], P[pidx][:],
                                                     masks[:, tt, e_i:e_i + 1],
                                                     xs[:, tt, :], AL.mult, AL.add),
                              waits=[tok, xs_tok[tt]])
                    pfree[pidx] = tacc
                    xs_tok[tt] = tacc
                    last_mm = tok
                wfree[slot] = last_mm

    if STAGE >= 2:
        emit_ffn(0, xs)


    if STAGE >= 3:
        bar3 = S.barrier()
        G = G_B
        TTg = TT // G
        Tg = TTg * 128
        xbspill = nc.dram_tensor("xbspill", [128, FC, TT, 128], BF16).ap()
        if DBG_XS:
            XDBG = nc.dram_tensor("xdbg", [128, TT, D], F32).ap()
            for tt in range(TT):
                dma_sp(XDBG[:, tt, :], xs[:, tt, :], waits=[xs_tok[tt]])
        prev_sp = None
        for tt in range(TT):
            tok, r_, mr_ = emit_ln(xs[:, tt, :], [xs_tok[tt]])
            th_ = ve(lambda e, tt=tt, r_=r_, mr_=mr_:
                     e.tensor_scalar(bscr[2][:], xs[:, tt, :], r_, mr_, AL.mult,
                                     AL.subtract),
                     waits=[tok] + ([prev_sp] if prev_sp else []))
            ln_prev[0] = th_
            t_s1 = dma_sp(xbspill[:, :, tt, :],
                          bscr[2][:].rearrange("p (c f) -> p c f", c=FC), waits=[th_])
            dma_sp(xspill[:, tt, :], xs[:, tt, :], waits=[xs_tok[tt], tok])
            prev_sp = t_s1
        t_xbsp = S.tok("sp")

        def Sv3(buf):
            return buf[:].rearrange("p (c s f) -> p c s f", c=FC, s=TTg)

        def Fv3(buf):
            return buf[:].rearrange("p (c t) -> p c t", c=FC)

        psz = PSZ_B
        PBv = lambda i: carve(i * psz, [128, FC * TTg * 128], BF16)
        xb_F = PBv(0); kf_sl = PBv(1); v_sl = PBv(2)
        tF = PBv(3); qf_t = PBv(4)
        h1_sl = PBv(5); sp1_sl = PBv(6); dpred_sl = PBv(7)
        dz2_sl = PBv(8); dz1_sl = PBv(9)
        brow32 = carve(10 * psz, [1, 2 * D], F32)
        dbacc = carve(10 * psz + 8, [1, 2 * D], F32)
        dstg_u = carve(10 * psz + 16, [128, D], F32)
        zrow = carve(10 * psz + 20, [128, D], F32)
        wboff = 10 * psz + 24
        wb = [carve(wboff + 16 * i, [128, FC, D], BF16) for i in range(3)]
        wlrb2 = carve(wboff + 48, [128, FC, 1], BF16)
        RB = {}
        for i, nm in enumerate(["bk", "bq", "bv", "b1", "b2"]):
            RB[nm] = carve(wboff + 49 + 2 * i, [1, D], BF16)
        RB["b1n"] = carve(wboff + 49, [1, D], BF16)
        RB["b2n"] = carve(wboff + 51, [1, D], BF16)
        ds1_u = carve(5 * psz, [128, D], F32)
        ds2_u = carve(5 * psz + 4, [128, D], F32)

        t_eps = ve(lambda e: e.memset(epscol[:], 1e-8), waits=list(bar3))
        t_zr = ve(lambda e: e.memset(zrow[:], 0.0), waits=list(bar3) + [t_xbsp])
        t_dbz = ve(lambda e: e.memset(dbacc[:], 0.0), waits=list(bar3) + [t_xbsp])
        t_wk = S.op("gpsimd", lambda e: e.dma_start(out=wb[0][:], in_=WK),
                    waits=list(bar3), inc="w0", n=16)
        t_wq = S.op("gpsimd", lambda e: e.dma_start(out=wb[1][:], in_=WQ),
                    waits=list(bar3), inc="w1", n=16)
        t_wv = S.op("gpsimd", lambda e: e.dma_start(out=wb[2][:], in_=WV),
                    waits=list(bar3), inc="w2", n=16)
        t_wlr = dma_w(wlrb2[:], WLR, waits=list(bar3))
        for nmk, srcb in (("bk", BK), ("bq", BQ), ("bv", BV), ("b1", B1), ("b2", B2)):
            dma_w(RB[nmk], srcb, waits=list(bar3))
        t_brows = S.tok("wl")
        t_b132 = dma_in2(brow32[:, 0:D], B1, waits=list(bar3) + [t_xbsp])
        t_b232 = dma_in2(brow32[:, D:2 * D], B2, waits=list(bar3) + [t_xbsp])

        def t_prod3(f_buf, f_toks, w_ap, w_tok, bias_row, TTr, evac_cb, extra=()):
            out_toks = []
            for ti in range(TTr):
                pidx = ti % 2
                fns = []
                for k in range(FC):
                    lhsT = Fv3(f_buf)[:, k, ti * 128:(ti + 1) * 128]
                    for nn in (0, 1):
                        fns.append(mmf(P[pidx][:, nn * 512:(nn + 1) * 512], lhsT,
                                       w_ap[:, k, nn * 512:(nn + 1) * 512], k == 0,
                                       k == FC - 1 and bias_row is None))
                if bias_row is not None:
                    for nn in (0, 1):
                        fns.append(mmf(P[pidx][:, nn * 512:(nn + 1) * 512],
                                       onesrb[:, 0:128],
                                       bias_row[:, nn * 512:(nn + 1) * 512], False,
                                       True))
                w = list(f_toks) + [w_tok, t_onesrb, t_brows] + pw(pidx) + list(extra)
                tok = mm_group(fns, waits=w)
                ctok = evac_cb(ti, P[pidx][:], tok)
                pfree[pidx] = ctok
                out_toks.append(ctok)
            return out_toks

        def transpose3(slab_buf, f_buf, slab_toks):
            toks = []
            for fc in range(FC):
                toks.append(dma_tp(
                    Fv3(f_buf)[:, fc, 0:Tg].rearrange("p (s f) -> p s f", s=TTg),
                    Sv3(slab_buf)[:, fc, 0:TTg, :], waits=list(slab_toks)))
            return toks

        r3 = lambda ap2d: ap2d.rearrange("p (c f) -> p c f", c=FC)
        nsl = Tg // 512 if Tg >= 512 else 1

        def f_prod3(rhs_buf, rhs_toks, w_ap, w_tok, bias_row, evac_cb, extra=()):
            out_toks = []
            for m in range(FC):
                pidx = m % 2
                fns = []
                nW = max(1, Tg // 512)
                wdt = 512 if Tg >= 512 else Tg
                for k in range(FC):
                    lhsT = w_ap[:, k, m * 128:(m + 1) * 128]
                    for nn in range(nW):
                        fns.append(mmf(P[pidx][:, nn * wdt:(nn + 1) * wdt], lhsT,
                                       Fv3(rhs_buf)[:, k, nn * wdt:(nn + 1) * wdt],
                                       k == 0, False))
                for nn in range(nW):
                    fns.append(mmf(P[pidx][:, nn * wdt:(nn + 1) * wdt],
                                   bias_row[:, m * 128:(m + 1) * 128],
                                   onesrb[:, nn * wdt:(nn + 1) * wdt], False, True))
                w = list(rhs_toks) + [w_tok, t_onesrb] + pw(pidx) + list(extra)
                tok = mm_group(fns, waits=w)
                ctok = evac_cb(m, P[pidx][:, 0:Tg], tok)
                pfree[pidx] = ctok
                out_toks.append(ctok)
            return out_toks

        rspill = nc.dram_tensor("rspill", [128, FC, TT * 128], BF16).ap()
        kfspill = nc.dram_tensor("kfspill", [128, FC, TT, 128], BF16).ap()
        vspill = nc.dram_tensor("vspill", [128, FC, TT, 128], BF16).ap()
        gW_dma = []
        prev_scr = None
        alr_all = []
        prevA = []
        for g in range(G):
            base = g * TTg
            tpb = []
            for fc in range(FC):
                tpb.append(dma_tp(
                    Fv3(xb_F)[:, fc, 0:Tg].rearrange("p (s f) -> p s f", s=TTg),
                    xbspill[:, fc, base:base + TTg, :],
                    waits=[t_xbsp] + prevA + list(bar3)))

            def evac_slab(dst_buf):
                def cb(ti, ps, tok):
                    return ve(lambda e, ti=ti: e.copy(Sv3(dst_buf)[:, :, ti, :], r3(ps)),
                              waits=[tok], eng="scalar")
                return cb
            kf_toks = t_prod3(xb_F, tpb, wb[0], t_wk, RB["bk"], TTg, evac_slab(kf_sl),
                              extra=[S.tok("sp")])
            v_toks = t_prod3(xb_F, tpb, wb[2], t_wv, RB["bv"], TTg, evac_slab(v_sl),
                             extra=[S.tok("sp")])
            alr_toks = []
            prev_alr = None
            for ti in range(TTg):
                tt = base + ti
                fns = [mmf(P[2][:, 0:1], Fv3(xb_F)[:, k, ti * 128:(ti + 1) * 128],
                           wlrb2[:, k, :], k == 0, k == FC - 1) for k in range(FC)]
                tok = mm_group(fns, waits=tpb + [t_wlr] + pw(2))
                t_sg = ve(lambda e, tt=tt: e.activation(alr2c[:, tt:tt + 1], P[2][:, 0:1],
                                                        AF.Sigmoid, bias=blrc_s[:]),
                          waits=[tok, ld["blrc"]] + ([prev_alr] if prev_alr else []),
                          eng="scalar")
                t_sc = ve(lambda e, tt=tt: e.tensor_scalar(alr2c[:, tt:tt + 1],
                                                           alr2c[:, tt:tt + 1],
                                                           0.1 * 2.0 / D, None, AL.mult),
                          waits=[t_sg])
                pfree[2] = t_sc
                prev_alr = t_sc
                alr_toks.append(t_sc)
            def evac_q(m, ps, tok):
                return ve(lambda e, m=m: e.copy(Fv3(qf_t)[:, m, :], ps), waits=[tok],
                          eng="scalar")
            q_toks = f_prod3(xb_F, tpb, wb[1], t_wq, RB["bq"], evac_q,
                             extra=[S.tok("sp")])
            for m in range(FC):
                dma_sp(qspill[:, m, base * 128:(base + TTg) * 128], Fv3(qf_t)[:, m, :],
                       waits=[q_toks[m]])
            dma_sp(kfspill[:, :, base:base + TTg, :], Sv3(kf_sl),
                   waits=list(kf_toks))
            dma_sp(vspill[:, :, base:base + TTg, :], Sv3(v_sl),
                   waits=list(v_toks))
            alr_all.append(alr_toks)
            prevA = [q_toks[-1]]
        t_qsp = S.tok("sp")
        tok_w1 = S.op("gpsimd", lambda e: e.dma_start(out=wb[0][:], in_=W1),
                      waits=[S.tok("mm")], inc="w0", n=16)
        tok_w2 = S.op("gpsimd", lambda e: e.dma_start(out=wb[1][:], in_=W2),
                      waits=[S.tok("mm")], inc="w1", n=16)
        tok_w2t = S.op("gpsimd", lambda e: e.dma_start(out=wb[2][:], in_=W2T),
                       waits=[S.tok("mm")], inc="w2", n=16)
        prev_h1 = prev_dz2 = prev_db1 = None
        for g in range(G):
            base = g * TTg
            alr_toks = alr_all[g]
            t_kfl = dma_in2(Sv3(kf_sl), kfspill[:, :, base:base + TTg, :],
                            waits=[S.tok("sp"), S.tok("tp")] +
                            ([prev_db1, prev_h1] if g > 0 else []))
            t_vl = dma_in2(Sv3(v_sl), vspill[:, :, base:base + TTg, :],
                           waits=[S.tok("sp")] + ([prev_dz2] if g > 0 else []))
            kf_toks = [t_kfl] * TTg
            v_toks = [t_vl] * TTg
            tkf = transpose3(kf_sl, tF, kf_toks)
            h1_toks = []

            def evac_z1(ti, ps, tok):
                w0 = [h1_toks[-1]] if h1_toks else []
                a0 = ve(lambda e: e.activation(bscr[1][:], ps, AF.Sigmoid),
                        waits=[tok] + w0, eng="scalar")
                a2 = ve(lambda e: e.activation(bscr[0][:], ps, AF.Silu), waits=[a0],
                        eng="scalar")
                m1 = ve(lambda e: e.tensor_tensor(bscr[2][:], bscr[0][:], bscr[1][:],
                                                  AL.mult), waits=[a2] + w0)
                m2 = ve(lambda e, ti=ti: e.tensor_tensor(Sv3(sp1_sl)[:, :, ti, :],
                                                         r3(bscr[1][:]),
                                                         r3(bscr[0][:]), AL.add),
                        waits=[m1])
                m3 = ve(lambda e, ti=ti: e.tensor_tensor(Sv3(sp1_sl)[:, :, ti, :],
                                                         Sv3(sp1_sl)[:, :, ti, :],
                                                         r3(bscr[2][:]), AL.subtract),
                        waits=[m2])
                a3 = ve(lambda e, ti=ti: e.tensor_tensor(Sv3(h1_sl)[:, :, ti, :],
                                                         Sv3(kf_sl)[:, :, ti, :],
                                                         r3(bscr[0][:]), AL.add),
                        waits=[m3, kf_toks[ti]])
                h1_toks.append(a3)
                return a3
            t_prod3(tF, tkf, wb[0], tok_w1, RB["b1"], TTg, evac_z1)
            th1 = transpose3(h1_sl, tF, h1_toks)
            dz2_toks = []

            def evac_z2(ti, ps, tok):
                tt = base + ti
                w0 = [dz2_toks[-1]] if dz2_toks else []
                a1 = ve(lambda e: e.activation(bscr[0][:], ps, AF.Silu),
                        waits=[tok] + w0, eng="scalar")
                a0 = ve(lambda e: e.activation(bscr[1][:], ps, AF.Sigmoid),
                        waits=[a1], eng="scalar")
                n1 = ve(lambda e: e.tensor_tensor(bscr[2][:], bscr[0][:], bscr[1][:],
                                                  AL.mult), waits=[a0] + w0)
                n2 = ve(lambda e: e.tensor_tensor(bscr[1][:], bscr[1][:], bscr[0][:],
                                                  AL.add), waits=[n1])
                a2 = ve(lambda e: e.tensor_tensor(bscr[1][:], bscr[1][:], bscr[2][:],
                                                  AL.subtract), waits=[n2])
                b1_ = ve(lambda e, ti=ti: e.tensor_tensor(r3(scr[0][:]),
                                                          Sv3(h1_sl)[:, :, ti, :],
                                                          r3(bscr[0][:]), AL.add),
                         waits=[a1, h1_toks[ti]] +
                         ([dz2_toks[-1]] if dz2_toks else []))
                b2_ = ve(lambda e, ti=ti: e.tensor_tensor(r3(scr[0][:]), r3(scr[0][:]),
                                                          Sv3(v_sl)[:, :, ti, :],
                                                          AL.subtract),
                         waits=[b1_, v_toks[ti]])
                b3_ = ve(lambda e, tt=tt, ti=ti: e.tensor_scalar(
                    Sv3(dpred_sl)[:, :, ti, :], r3(scr[0][:]), alr2c[:, tt:tt + 1],
                    None, AL.mult), waits=[b2_, alr_toks[ti]])
                b4_ = ve(lambda e, ti=ti: e.tensor_tensor(Sv3(dz2_sl)[:, :, ti, :],
                                                          Sv3(dpred_sl)[:, :, ti, :],
                                                          r3(bscr[1][:]), AL.mult),
                         waits=[b3_, a2])
                dz2_toks.append(b4_)
                return b4_
            t_prod3(tF, th1, wb[1], tok_w2, RB["b2"], TTg, evac_z2)
            for m in range(FC):
                pidx = m % 2
                fns = []
                for ti in range(TTg):
                    lhsT = Sv3(h1_sl)[:, m, ti, :]
                    for half in (0, 1):
                        fns.append(mmf(P[pidx][:, half * 512:(half + 1) * 512], lhsT,
                                       Sv3(dz2_sl)[:, half * 4:(half + 1) * 4, ti, :],
                                       ti == 0, ti == TTg - 1))
                tok = mm_group(fns, waits=dz2_toks + h1_toks + pw(pidx))
                tev = ve(lambda e, pidx=pidx: e.tensor_copy(scr[1][:], P[pidx][:]),
                         waits=[tok, S.tok("wl")])
                tdm = dma_w(ar_in[:, 8 + m, :], scr[1][:], waits=[tev], accum=(g > 0))
                gW_dma.append(tdm)
                pfree[pidx] = tdm
            fns = []
            for ti in range(TTg):
                for half in (0, 1):
                    fns.append(mmf(P[2][0:1, half * 512:(half + 1) * 512], onescb[:],
                                   Sv3(dz2_sl)[:, half * 4:(half + 1) * 4, ti, :],
                                   ti == 0, ti == TTg - 1))
            t_db2 = mm_group(fns, waits=dz2_toks + [t_onescb] + pw(2))
            acc2 = ve(lambda e: e.tensor_tensor(dbacc[0:1, D:2 * D],
                                                dbacc[0:1, D:2 * D],
                                                P[2][0:1, :], AL.add),
                      waits=[t_db2, (acc2_prev if g > 0 else t_dbz)])
            acc2_prev = acc2
            pfree[2] = acc2
            if g == 0:
                t_z16 = dma_w(ar_in[:, 16, :], zrow[:], waits=[t_zr])
                gW_dma.append(t_z16)
            tdz2 = transpose3(dz2_sl, tF, dz2_toks)
            dz1_toks = []

            def evac_dh1(ti, ps, tok):
                u1 = ve(lambda e, ti=ti: e.scalar_tensor_tensor(
                    r3(scr[0][:]), r3(ps), 1.0, Sv3(dpred_sl)[:, :, ti, :], AL.mult,
                    AL.add), waits=[tok, S.tok("sp")] +
                    ([dz1_toks[-1]] if dz1_toks else []))
                u2 = ve(lambda e, ti=ti: e.tensor_tensor(Sv3(dz1_sl)[:, :, ti, :],
                                                         r3(scr[0][:]),
                                                         Sv3(sp1_sl)[:, :, ti, :],
                                                         AL.mult), waits=[u1])
                dz1_toks.append(u2)
                return u2
            t_prod3(tF, tdz2, wb[2], tok_w2t, None, TTg, evac_dh1)
            for m in range(FC):
                pidx = m % 2
                fns = []
                for ti in range(TTg):
                    lhsT = Sv3(kf_sl)[:, m, ti, :]
                    for half in (0, 1):
                        fns.append(mmf(P[pidx][:, half * 512:(half + 1) * 512], lhsT,
                                       Sv3(dz1_sl)[:, half * 4:(half + 1) * 4, ti, :],
                                       ti == 0, ti == TTg - 1))
                tok = mm_group(fns, waits=dz1_toks + kf_toks + pw(pidx))
                tev = ve(lambda e, pidx=pidx: e.tensor_copy(scr[1][:], P[pidx][:]),
                         waits=[tok, S.tok("wl")])
                tdm = dma_w(ar_in[:, m, :], scr[1][:], waits=[tev], accum=(g > 0))
                gW_dma.append(tdm)
                pfree[pidx] = tdm
            fns = []
            for ti in range(TTg):
                for half in (0, 1):
                    fns.append(mmf(P[3][0:1, half * 512:(half + 1) * 512], onescb[:],
                                   Sv3(dz1_sl)[:, half * 4:(half + 1) * 4, ti, :],
                                   ti == 0, ti == TTg - 1))
            t_db1 = mm_group(fns, waits=dz1_toks + [t_onescb] + pw(3))
            acc1 = ve(lambda e: e.tensor_tensor(dbacc[0:1, 0:D],
                                                dbacc[0:1, 0:D],
                                                P[3][0:1, :], AL.add),
                      waits=[t_db1, (acc1_prev if g > 0 else t_dbz)])
            acc1_prev = acc1
            pfree[3] = acc1
            prev_scr = dz1_toks[-1]
            prev_h1 = h1_toks[-1]
            prev_dz2 = dz2_toks[-1]
            prev_db1 = t_db1

        tdm1 = dma_w(ar_in[0:1, 16, :], dbacc[0:1, 0:D],
                     waits=[acc1_prev, S.tok("wl")])
        tdm2 = dma_w(ar_in[1:2, 16, :], dbacc[0:1, D:2 * D],
                     waits=[acc2_prev])
        gW_dma += [tdm1, tdm2]

        t_ar = S.op("gpsimd",
                    lambda e: e.collective_compute(
                        "AllReduce", AL.add, replica_groups=[list(range(NCORES))],
                        ins=[ar_in.opt()], outs=[ar_out.opt()]),
                    waits=[S.tok("wl")], inc="cc")
        upd_toks = {}
        prev_u = None
        for wi, (slot_old, slot_new, aro) in enumerate(
                ((0, 0, 0), (1, 1, 8))):
            for k in range(FC):
                t_g = dma_in2(dstg_u[:], ar_out[:, aro + k, :],
                             waits=[t_ar] + ([prev_u] if prev_u else []))
                p1 = ve(lambda e: e.activation(ds1_u[:], dstg_u[:], AF.Abs),
                        waits=[t_g], eng="scalar")
                p2 = ve(lambda e: e.activation(ds2_u[:], ds1_u[:], AF.Ln, bias=epscol[:]),
                        waits=[p1, t_eps], eng="scalar")
                p3 = ve(lambda e: e.tensor_scalar(ds2_u[:], ds2_u[:], -1.0, None, AL.mult),
                        waits=[p2])
                p4 = ve(lambda e: e.activation(ds1_u[:], ds2_u[:], AF.Exp), waits=[p3],
                        eng="scalar")
                p5 = ve(lambda e: e.scalar_tensor_tensor(ds2_u[:], dstg_u[:], 0.01,
                                                         ds1_u[:], AL.mult, AL.mult),
                        waits=[p4])
                p6 = ve(lambda e, slot_old=slot_old, slot_new=slot_new, k=k:
                        e.scalar_tensor_tensor(wb[slot_new][:, k, :],
                                               wb[slot_old][:, k, :], 0.9999, ds2_u[:],
                                               AL.mult, AL.subtract), waits=[p5])
                prev_u = p6
            upd_toks[wi] = prev_u
        prev_b = None
        for bup, nm in ((0, "b1n"), (1, "b2n")):
            t_g = dma_in2(scr[1][0:1, :], ar_out[bup:bup + 1, 16, :],
                         waits=[t_ar] + ([prev_b] if prev_b else []) + [prev_u])
            q1 = ve(lambda e: e.activation(scr[2][0:1, :], scr[1][0:1, :], AF.Abs),
                    waits=[t_g], eng="scalar")
            q2 = ve(lambda e: e.activation(scr[2][0:1, :], scr[2][0:1, :], AF.Ln,
                                           bias=epscol[0:1, :]), waits=[q1], eng="scalar")
            q3 = ve(lambda e: e.tensor_scalar(scr[2][0:1, :], scr[2][0:1, :], -1.0, None,
                                              AL.mult), waits=[q2])
            q4 = ve(lambda e: e.activation(scr[2][0:1, :], scr[2][0:1, :], AF.Exp),
                    waits=[q3], eng="scalar")
            q5 = ve(lambda e: e.scalar_tensor_tensor(scr[2][0:1, :], scr[1][0:1, :], 0.01,
                                                     scr[2][0:1, :], AL.mult, AL.mult),
                    waits=[q4])
            q6 = ve(lambda e, bup=bup, nm=nm: e.scalar_tensor_tensor(
                RB[nm], brow32[:, bup * D:(bup + 1) * D], 0.9999, scr[2][0:1, :],
                AL.mult, AL.subtract), waits=[q5, t_b132, t_b232])
            prev_b = q6
        t_bn = prev_b
        for g in range(G):
            base = g * TTg
            t_ql = dma_in2(Fv3(qf_t), qspill[:, :, base * 128:(base + TTg) * 128],
                          waits=[S.tok("sp"), upd_toks[0], t_bn] +
                          ([prev_scr] if prev_scr else []))
            r1_toks = []

            def evac_r1(m, ps, tok):
                a1 = ve(lambda e: e.activation(bscr[0][:, 0:Tg], ps, AF.Silu),
                        waits=[tok] + ([r1_toks[-1]] if r1_toks else []), eng="scalar")
                a2 = ve(lambda e, m=m: e.tensor_tensor(Fv3(tF)[:, m, :],
                                                       Fv3(qf_t)[:, m, :],
                                                       bscr[0][:, 0:Tg], AL.add),
                        waits=[a1, t_ql])
                r1_toks.append(a2)
                return a2
            f_prod3(qf_t, [t_ql], wb[0], upd_toks[0], RB["b1n"], evac_r1,
                    extra=[t_bn])
            r2_toks = []

            def evac_r2(m, ps, tok):
                a1 = ve(lambda e: e.activation(bscr[0][:, 0:Tg], ps, AF.Silu),
                        waits=[tok] + ([r2_toks[-1]] if r2_toks else []) +
                        ([r1_toks[-1]] if r1_toks else []), eng="scalar")
                a2 = ve(lambda e, m=m: e.tensor_tensor(Fv3(xb_F)[:, m, :],
                                                       Fv3(tF)[:, m, :],
                                                       bscr[0][:, 0:Tg], AL.add),
                        waits=[a1] + r1_toks)
                r2_toks.append(a2)
                return a2
            f_prod3(tF, r1_toks, wb[1], upd_toks[1], RB["b2n"], evac_r2,
                    extra=[t_bn])
            for m in range(FC):
                dma_sp(rspill[:, m, base * 128:(base + TTg) * 128], Fv3(xb_F)[:, m, :],
                       waits=[r2_toks[m]])
            prev_scr = r2_toks[-1]
        bar4 = S.barrier()
        xs2 = carve(0, [128, TT, D], F32)
        r2sl = carve(XS_KB, [128, FC * TT * 128], BF16)
        r2v = r2sl[:].rearrange("p (c s f) -> p c s f", c=FC, s=TT)
        tp_r2 = []
        for fc in range(FC):
            tp_r2.append(dma_tp(
                r2v[:, fc, :, :],
                rspill[:, fc, :].rearrange("p (s f) -> p s f", s=TT) if False else
                rspill[:, fc, :],
                waits=list(bar4) + [S.tok("sp")]))
        for tt in range(TT):
            t_rl = dma_in2(xs2[:, tt, :], xspill[:, tt, :], waits=list(bar4) +
                          [S.tok("sp")])
            t_ad = ve(lambda e, tt=tt: e.tensor_tensor(
                xs2[:, tt, :].rearrange("p (c f) -> p c f", c=FC),
                xs2[:, tt, :].rearrange("p (c f) -> p c f", c=FC),
                r2v[:, :, tt, :], AL.add),
                waits=[t_rl] + tp_r2)
            xs_tok[tt] = t_ad
        xs_fin = xs2
    else:
        xs_fin = xs

    if STAGE >= 4:
        emit_ffn(1, xs_fin)

    st_toks = [dma_st(XOUT[:, tt, :], xs_fin[:, tt, :], waits=[xs_tok[tt]])
               for tt in range(TT)]
    S.op("sync", lambda e: e.nop(), waits=[S.tok("st")])

    with es, nc.Block() as block:
        def mk(ename):
            def body(eng):
                for waits, fn, inc, n in S.ops[ename]:
                    for s, v in waits:
                        eng.wait_ge(sems[s], v)
                    ins = fn(eng)
                    if inc is not None:
                        ins.then_inc(sems[inc], n)
            return body
        block.sync(mk("sync"))
        block.tensor(mk("tensor"))
        block.vector(mk("vector"))
        block.scalar(mk("scalar"))
        block.gpsimd(mk("gpsimd"))
    nc._sched_val = dict(S.val)
    return nc


def _consts():
    B = 128
    nfft = 256
    kk = np.arange(B + 1)
    jj = np.arange(B)
    ang = 2 * np.pi * np.outer(jj, kk) / nfft
    cick = np.where((kk == 0) | (kk == B), 1.0, 2.0)
    angi = 2 * np.pi * np.outer(kk, jj) / nfft
    wc_full = (cick[:, None] * np.cos(angi) / nfft).astype(np.float32)
    ws_full = (-cick[:, None] * np.sin(angi) / nfft).astype(np.float32)
    return {
        "fc129": np.cos(ang).astype(np.float32),
        "fs128": (-np.sin(ang[:, 0:128])).astype(np.float32),
        "wc": wc_full[0:128], "ws": ws_full[0:128], "wcx": wc_full[128:129],
        "dramp": np.tile(np.arange(129, dtype=np.float32), (N64, 1)),
        "rampt": np.tile((127.0 - np.arange(128, dtype=np.float32))[:, None], (1, N64)),
        "ramp16c": np.tile(128.0 * np.arange(16, dtype=np.float32)[None, :], (N64, 1)),
        "ones_r": np.ones((1, 2048), np.float32),
        "ones_c": np.ones((128, 1), np.float32),
    }


def _prep_inputs(inputs, TT):
    T_core = TT * 128
    x = np.asarray(inputs["L0_attn_a_x"][0], np.float32)
    base = dict(_consts())
    base.update({
        "lrc": np.asarray(inputs["L0_attn_a_LR"], np.float32).reshape(N64, 1),
        "lic": np.asarray(inputs["L0_attn_a_LI"], np.float32).reshape(N64, 1),
        "lrr": np.asarray(inputs["L0_attn_a_LR"], np.float32).reshape(1, N64),
        "lir": np.asarray(inputs["L0_attn_a_LI"], np.float32).reshape(1, N64),
        "crt": np.ascontiguousarray(np.asarray(inputs["L0_attn_a_CR"], np.float32).T),
        "cit": np.ascontiguousarray(np.asarray(inputs["L0_attn_a_CI"], np.float32).T),
        "dmap": np.tile(np.asarray(inputs["L0_attn_a_D"], np.float32)[None, :], (128, 1)),
        "gmap": np.tile(np.asarray(inputs["L0_attn_a_gamma"], np.float32)[None, :],
                        (128, 1)),
        "bmapa": np.tile(np.asarray(inputs["L0_attn_a_beta"], np.float32)[None, :],
                         (128, 1)),
        "blrc": np.tile(np.asarray(inputs["L0_attn_b_blr"], np.float32).reshape(1, 1),
                        (128, 1)),
        "gwt": np.ascontiguousarray(
            np.asarray(inputs["L0_ffn_gate_w"], np.float32).T.reshape(FC, 128, 8)
            .transpose(1, 0, 2)),
        "ew": np.ascontiguousarray(
            np.asarray(inputs["L0_ffn_expert_w"], np.float32).transpose(0, 2, 1)
            .reshape(8, FC, 128, D).transpose(0, 2, 1, 3)),
        "wlr": np.ascontiguousarray(
            np.asarray(inputs["L0_attn_b_Wlr"], np.float32).reshape(FC, 128, 1)
            .transpose(1, 0, 2)),
    })
    for nm, key in (("wk", "Wk"), ("wq", "Wq"), ("wv", "Wv"), ("w1", "W1"), ("w2", "W2")):
        W = np.asarray(inputs[f"L0_attn_b_{key}"], np.float32)
        base[nm] = np.ascontiguousarray(W.reshape(FC, 128, D).transpose(1, 0, 2))
    W2t = np.ascontiguousarray(np.asarray(inputs["L0_attn_b_W2"], np.float32).T)
    base["w2t"] = np.ascontiguousarray(W2t.reshape(FC, 128, D).transpose(1, 0, 2))
    for nm in ("bk", "bq", "bv", "b1", "b2"):
        base[nm] = np.asarray(inputs[f"L0_attn_b_{nm}"], np.float32).reshape(1, D)

    in_maps = []
    for c in range(NCORES):
        xc = x[c * T_core:(c + 1) * T_core].reshape(TT, 128, D).transpose(1, 0, 2)
        eoff = np.full((N64, NCORES), 1e9, np.float32)
        for b in range(c):
            eoff[:, b] = float(T_core) * (c - 1 - b)
        m = dict(base)
        m["x"] = np.ascontiguousarray(xc)
        m["eoff"] = eoff
        in_maps.append(m)
    return in_maps


_NC_CACHE = {}


def kernel(**inputs):
    TT = TTK
    in_maps = _prep_inputs(inputs, TT)
    if TT not in _NC_CACHE:
        _NC_CACHE[TT] = build_nc(TT)
    nc = _NC_CACHE[TT]
    res = run_bass_kernel_spmd(nc, in_maps, core_ids=list(range(NCORES)))
    T_core = TT * 128
    out = np.empty((1, NCORES * T_core, D), np.float32)
    for c in range(NCORES):
        xo = np.asarray(res.results[c]["xout"])
        out[0, c * T_core:(c + 1) * T_core] = xo.transpose(1, 0, 2).reshape(T_core, D)
    return out
